# version 12
# baseline (speedup 1.0000x reference)
import numpy as np
import ml_dtypes
from contextlib import ExitStack

import concourse.bass as bass
import concourse.mybir as mybir
from concourse.bass_utils import run_bass_kernel_spmd

F32 = mybir.dt.float32
BF16 = mybir.dt.bfloat16
AF = mybir.ActivationFunctionType
ALU = mybir.AluOpType
AX = mybir.AxisListType

L, DM, DI, NST, DR, KC = 4096, 1024, 2048, 16, 64, 4
TC = 256
NCH = L // TC
NTT = TC // 128
NDB = DI // 128
NKD = DM // 128
NMI = 2 * DI // 128
EPS_LN = 1e-5
EPS_FFN = 1e-5
DBG = False

ENGS = ("sync", "tensor", "vector", "scalar", "gpsimd")


class Em:

    def __init__(self):
        self.ops = {e: [] for e in ENGS}
        self.cnt = {}
        self.last = {e: {} for e in ENGS}

    def sem(self, name):
        self.cnt[name] = 0
        return name

    def wait(self, eng, sem, val=None):
        v = self.cnt[sem] if val is None else val
        if v <= 0:
            return
        if self.last[eng].get(sem, 0) >= v:
            return
        self.last[eng][sem] = v
        self.ops[eng].append(("wait", sem, v))

    def op(self, eng, fn, inc=None, k=1):
        if inc is not None:
            self.cnt[inc] += k
        self.ops[eng].append(("op", fn, inc, k))


def build_program():
    nc = bass.Bass("TRN2", target_bir_lowering=False, debug=False)

    x_h = nc.dram_tensor("x", [L, DM], F32, kind="ExternalInput").ap()
    win_h = nc.dram_tensor("w_in", [2 * DI, DM], F32, kind="ExternalInput").ap()
    conw_h = nc.dram_tensor("conv_w", [DI, KC], F32, kind="ExternalInput").ap()
    conb_h = nc.dram_tensor("conv_b", [DI, 1], F32, kind="ExternalInput").ap()
    wx_h = nc.dram_tensor("w_x", [DR + 2 * NST, DI], F32, kind="ExternalInput").ap()
    wdt_h = nc.dram_tensor("w_dt", [DI, DR], F32, kind="ExternalInput").ap()
    bdt_h = nc.dram_tensor("b_dt", [DI, 1], F32, kind="ExternalInput").ap()
    alog_h = nc.dram_tensor("a_log", [DI, NST], F32, kind="ExternalInput").ap()
    dpar_h = nc.dram_tensor("d_par", [DI, 1], F32, kind="ExternalInput").ap()
    wout_h = nc.dram_tensor("w_out", [DM, DI], F32, kind="ExternalInput").ap()
    wffn_h = nc.dram_tensor("w_ffn", [DM, DM], F32, kind="ExternalInput").ap()
    ident_h = nc.dram_tensor("ident", [128, 128], BF16, kind="ExternalInput").ap()
    selB_h = nc.dram_tensor("selB", [2 * NST, NST * 128], BF16, kind="ExternalInput").ap()
    selC_h = nc.dram_tensor("selC", [2 * NST, NST * 128], BF16, kind="ExternalInput").ap()
    onesf_h = nc.dram_tensor("onesf", [1, 128], F32, kind="ExternalInput").ap()
    onesc_h = nc.dram_tensor("onesc", [128, 1], F32, kind="ExternalInput").ap()
    out_h = nc.dram_tensor("out", [L, DM], F32, kind="ExternalOutput").ap()

    em = Em()
    LD, ST, TP, VP, SP, GP = (em.sem(s) for s in ("LD", "ST", "TP", "VP", "SP", "GP"))

    with ExitStack() as ctx:
        sb = lambda *a: ctx.enter_context(nc.sbuf_tensor(*a))
        psum = lambda *a: ctx.enter_context(nc.psum_tensor(*a))

        winT = sb("winT", [128, NKD * (2 * DI)], BF16)
        woutT = sb("woutT", [128, NDB * DM], BF16)
        ffnT = sb("ffnT", [128, NKD * DM], BF16)
        wxT = sb("wxT", [128, NDB * 96], BF16)
        wdtT = sb("wdtT", [64, DI], BF16)
        conw = sb("conw", [128, NDB * KC], F32)
        conb = sb("conb", [128, NDB], F32)
        bdt = sb("bdt", [128, NDB], F32)
        dpar = sb("dpar", [128, NDB], F32)
        aneg = sb("aneg", [128, NDB * NST], F32)
        selB = sb("selB_s", [2 * NST, NST * 128], BF16)
        selC = sb("selC_s", [2 * NST, NST * 128], BF16)
        identb = sb("identb", [128, 128], BF16)
        onesf = sb("onesf_s", [1, 128], F32)
        onesc = sb("onesc_s", [128, 1], F32)
        sth = sb("sth", [128, 4], F32)
        s_t = sb("s_t", [1, 4], F32)
        absacc = sb("absacc", [128, NKD + 1], F32)

        scrx = sb("scrx", [128, NTT * DM], F32)
        lnstat = sb("lnstat", [128, 16], F32)
        xn_tm = sb("xn_tm", [128, DM], BF16)
        xn_fm = sb("xn_fm", [128, NKD * TC], BF16)
        wstg_f = scrx
        wstg_b = xn_fm
        xpbuf = sb("xpbuf", [128, NDB * (TC + KC)], BF16)
        convacc = sb("convacc", [128, 2 * TC], F32)
        z_sb = sb("z_sb", [128, NDB * TC], BF16)
        u_sb = sb("u_sb", [128, NDB * TC], BF16)
        delta = sb("delta", [128, NDB * TC], BF16)
        dlt_sb = sb("dlt_sb", [64, TC], BF16)
        bc_sb = sb("bc_sb", [32, TC], BF16)
        e_scr = sb("e_scr", [128, 2 * TC], F32)
        du_sb = sb("du_sb", [128, TC], BF16)
        a_sb = sb("a_sb", [128, 2 * TC], F32)
        b_sb = sb("b_sb", [128, 2 * TC], BF16)
        Ball = sb("Ball", [128, NST * TC], BF16)
        Call = sb("Call", [128, NST * TC], BF16)
        h_sb = sb("h_sb", [128, 2 * TC], BF16)
        g_sb = sb("g_sb", [128, 2 * TC], BF16)
        yp_sb = sb("yp_sb", [128, 2 * TC], BF16)
        yp2_sb = sb("yp2_sb", [128, TC], BF16)
        yacc = sb("yacc", [128, TC], F32)
        y2_sb = sb("y2_sb", [128, TC], BF16)
        carry = sb("carry", [128, NDB * NST], F32)
        ypost = delta
        out_sb = Ball[:].bitcast(F32)

        ps_mm = [psum("ps_mmA", [128, 512], F32), psum("ps_mmB", [128, 512], F32)]
        ps_bcB = psum("ps_bcB", [128, 512], F32)
        ps_bcC = psum("ps_bcC", [128, 512], F32)
        ps_out = psum("ps_out", [128, 1024], F32)
        ps_tr = [psum("ps_trA", [128, 1024], BF16), psum("ps_trB", [128, 1024], BF16)]

        def dma(eng, out_ap, in_ap, sem=LD):
            em.op(eng, (lambda o=out_ap, i=in_ap: (lambda e: e.dma_start(out=o, in_=i)))(), inc=sem, k=16)

        dbg_handles = {}

        vdum = sb("vdum", [128, 4], F32)

        def settle(eng, sem):
            for _ in range(2):
                if eng == "vector":
                    em.op("vector", lambda e: e.memset(vdum[:, 0:2], 0.0), inc=sem)
                elif eng == "scalar":
                    em.op("scalar", lambda e: e.activation(out=vdum[:, 2:3], in_=vdum[:, 3:4], func=AF.Copy), inc=sem)

        def dbg(name, src_ap, psem, pval):
            if not DBG:
                return
            dt = src_ap.dtype
            shp = list(src_ap.shape)
            h = nc.dram_tensor("dbg_" + name, shp, dt, kind="ExternalOutput").ap()
            dbg_handles[name] = h
            em.wait("gpsimd", psem, pval)
            dma("gpsimd", h, src_ap, sem=ST)
            stv = em.cnt[ST]
            for e in ("sync", "tensor", "vector", "scalar"):
                em.wait(e, ST, stv)

        for d in range(NDB):
            r = slice(d * 128, (d + 1) * 128)
            dma("sync", conw[:, d * KC:(d + 1) * KC], conw_h[r, :])
            dma("sync", conb[:, d:d + 1], conb_h[r, :])
            dma("sync", bdt[:, d:d + 1], bdt_h[r, :])
            dma("sync", dpar[:, d:d + 1], dpar_h[r, :])
            dma("sync", aneg[:, d * NST:(d + 1) * NST], alog_h[r, :])
        dma("sync", identb[:], ident_h[:])
        dma("sync", selB[:], selB_h[:])
        dma("sync", selC[:], selC_h[:])
        dma("sync", onesf[:], onesf_h[:])
        dma("sync", onesc[:], onesc_h[:])
        ld_consts = em.cnt[LD]

        em.wait("scalar", LD, ld_consts)
        em.op("scalar", lambda e: e.activation(out=aneg[:], in_=aneg[:], func=AF.Exp), inc=SP)
        em.wait("vector", SP)
        em.op("vector", lambda e: e.tensor_scalar_mul(out=aneg[:], in0=aneg[:], scalar1=-1.0), inc=VP)
        em.op("vector", lambda e: e.memset(carry[:], 0.0), inc=VP)
        em.op("vector", lambda e: e.memset(xpbuf[:], 0.0), inc=VP)

        tr_par = [0]
        tr_busy = {0: 0, 1: 0}
        stg_state = {"conv_sp": 0, "tr_tp": 0}

        def prep_block(src_ap, ncols, nkb, evac_fn, stg_off):
            rows = src_ap.shape[0]
            fcols = slice(stg_off, stg_off + ncols)
            em.wait("sync", SP, stg_state["conv_sp"])
            dma("sync", wstg_f[0:rows, fcols], src_ap)
            ldv = em.cnt[LD]
            em.wait("scalar", LD, ldv)
            em.wait("scalar", TP, stg_state["tr_tp"])
            em.op("scalar", (lambda r=rows, fc=fcols: (lambda e: e.activation(
                out=wstg_b[0:r, fc], in_=wstg_f[0:r, fc], func=AF.Copy)))(), inc=SP)
            stg_state["conv_sp"] = em.cnt[SP]
            em.wait("tensor", SP, stg_state["conv_sp"])
            for kb in range(nkb):
                par = tr_par[0]
                tr_par[0] ^= 1
                em.wait("tensor", SP, tr_busy[par])
                c0 = stg_off + kb * 128
                w = min(ncols - kb * 128, 128)
                em.op("tensor", (lambda r=rows, c0=c0, p=par, w=w: (lambda e: e.transpose(
                    out=ps_tr[p][0:w, 0:r], in_=wstg_b[0:r, c0:c0 + w],
                    identity=identb[0:r, 0:r])))(), inc=TP)
                stg_state["tr_tp"] = em.cnt[TP]
                em.wait("scalar", TP, stg_state["tr_tp"])
                out_ap, hh, ww = evac_fn(kb)
                em.op("scalar", (lambda o=out_ap, p=par, hh=hh, ww=ww: (lambda e: e.activation(
                    out=o, in_=ps_tr[p][0:hh, 0:ww], func=AF.Copy)))(), inc=SP)
                tr_busy[par] = em.cnt[SP]

        for j in range(NMI):
            prep_block(win_h[j * 128:(j + 1) * 128, :], DM, NKD,
                       (lambda j=j: lambda kb: (
                           winT[:, kb * (2 * DI) + j * 128: kb * (2 * DI) + (j + 1) * 128], 128, 128))(),
                       (j % 2) * 1024)
        for j in range(NKD):
            prep_block(wout_h[j * 128:(j + 1) * 128, :], DI, NDB,
                       (lambda j=j: lambda kb: (
                           woutT[:, kb * DM + j * 128: kb * DM + (j + 1) * 128], 128, 128))(),
                       0)
        prep_block(wx_h[:, :], DI, NDB,
                   lambda kb: (wxT[:, kb * 96:(kb + 1) * 96], 128, 96), 0)
        for j in range(NDB):
            prep_block(wdt_h[j * 128:(j + 1) * 128, :], DR, 1,
                       (lambda j=j: lambda kb: (wdtT[0:64, j * 128:(j + 1) * 128], 64, 128))(),
                       (j % 2) * 1024)

        for j in range(NKD):
            fc = slice((j % 2) * 1024, (j % 2) * 1024 + 1024)
            em.wait("sync", SP, stg_state["conv_sp"])
            em.wait("sync", VP, stg_state.get("ffn_vp", 0))
            dma("sync", wstg_f[:, fc], wffn_h[j * 128:(j + 1) * 128, :])
            ldv = em.cnt[LD]
            em.wait("vector", LD, ldv)
            em.op("vector", lambda e: e.nop(cycle_cnt=10000), inc=VP)
            em.op("vector", (lambda j=j, fc=fc: (lambda e: e.tensor_reduce(
                out=absacc[:, j:j + 1], in_=wstg_f[:, fc],
                axis=AX.X, op=ALU.add, apply_absolute_value=True)))(), inc=VP)
            if j % 2 == 1:
                stg_state["ffn_vp"] = em.cnt[VP]
        settle("vector", VP)
        em.op("vector", lambda e: e.tensor_reduce(
            out=absacc[:, NKD:NKD + 1], in_=absacc[:, 0:NKD], axis=AX.X, op=ALU.add), inc=VP)
        settle("vector", VP)
        em.wait("tensor", VP)
        em.wait("tensor", SP, em.cnt[SP])
        em.op("tensor", lambda e: e.matmul(out=ps_mm[0][0:1, 0:1], lhsT=absacc[:, NKD:NKD + 1],
                                           rhs=onesc[:, 0:1], start=True, stop=True), inc=TP)
        em.wait("scalar", TP)
        em.op("scalar", lambda e: e.activation(out=s_t[0:1, 0:1], in_=ps_mm[0][0:1, 0:1], func=AF.Copy), inc=SP)
        ffn_mm0_free = em.cnt[SP]
        settle("scalar", SP)
        em.wait("vector", SP)
        em.op("vector", lambda e: e.tensor_scalar(
            out=s_t[0:1, 2:3], in0=s_t[0:1, 0:1], scalar1=-0.5 / (DM * DM),
            scalar2=-0.5 * EPS_FFN, op0=ALU.mult, op1=ALU.min), inc=VP)
        settle("vector", VP)
        em.wait("tensor", VP)
        em.op("tensor", lambda e: e.matmul(out=ps_mm[1][:, 0:1], lhsT=onesf[0:1, :],
                                           rhs=s_t[0:1, 2:3], start=True, stop=True), inc=TP)
        em.wait("scalar", TP)
        em.op("scalar", lambda e: e.activation(out=sth[:, 0:1], in_=ps_mm[1][:, 0:1], func=AF.Copy), inc=SP)
        ffn_mm1_free = em.cnt[SP]
        settle("scalar", SP)
        em.wait("vector", SP)
        em.op("vector", lambda e: e.tensor_scalar_mul(out=sth[:, 1:2], in0=sth[:, 0:1], scalar1=-1.0), inc=VP)
        em.op("vector", lambda e: e.tensor_scalar_mul(out=sth[:, 2:3], in0=sth[:, 0:1], scalar1=2.0), inc=VP)
        settle("vector", VP)
        vp_sth = em.cnt[VP]

        for j in range(NKD):
            fc = slice((j % 2) * 1024, (j % 2) * 1024 + 1024)
            em.wait("sync", SP, stg_state["conv_sp"])
            em.wait("sync", VP, stg_state.get("ffnB_vp", vp_sth))
            dma("sync", wstg_f[:, fc], wffn_h[j * 128:(j + 1) * 128, :])
            ldv = em.cnt[LD]
            em.wait("vector", LD, ldv)
            em.wait("vector", VP, vp_sth)
            em.op("vector", lambda e: e.nop(cycle_cnt=10000), inc=VP)
            em.op("vector", (lambda fc=fc: (lambda e: e.tensor_scalar(
                out=u_sb[:, 0:1024], in0=wstg_f[:, fc], scalar1=sth[:, 1:2], scalar2=None,
                op0=ALU.is_gt)))(), inc=VP)
            em.op("vector", (lambda fc=fc: (lambda e: e.scalar_tensor_tensor(
                out=u_sb[:, 1024:2048], in0=wstg_f[:, fc], scalar=sth[:, 0:1],
                in1=u_sb[:, 0:1024], op0=ALU.is_lt, op1=ALU.subtract)))(), inc=VP)
            if j % 2 == 1:
                stg_state["ffnB_vp"] = em.cnt[VP]
            em.wait("vector", TP, stg_state["tr_tp"])
            em.op("vector", (lambda fc=fc: (lambda e: e.tensor_scalar(
                out=wstg_b[:, fc], in0=u_sb[:, 1024:2048], scalar1=sth[:, 2:3], scalar2=None,
                op0=ALU.mult)))(), inc=VP)
            vpv = em.cnt[VP]
            em.wait("tensor", VP, vpv)
            for kb in range(NKD):
                par = tr_par[0]
                tr_par[0] ^= 1
                em.wait("tensor", SP, tr_busy[par])
                em.op("tensor", (lambda kb=kb, p=par, fc=fc: (lambda e: e.transpose(
                    out=ps_tr[p][:, 0:128], in_=wstg_b[:, fc.start + kb * 128: fc.start + (kb + 1) * 128],
                    identity=identb[:])))(), inc=TP)
                stg_state["tr_tp"] = em.cnt[TP]
                em.wait("scalar", TP, stg_state["tr_tp"])
                em.op("scalar", (lambda kb=kb, j=j, p=par: (lambda e: e.activation(
                    out=ffnT[:, kb * DM + j * 128: kb * DM + (j + 1) * 128],
                    in_=ps_tr[p][:, 0:128], func=AF.Copy)))(), inc=SP)
                tr_busy[par] = em.cnt[SP]
        vp_prep_end = em.cnt[VP]

        dbg("absacc", absacc[:], VP, em.cnt[VP])
        dbg("sth", sth[:], VP, em.cnt[VP])
        dbg("winT", winT[:], SP, em.cnt[SP])
        dbg("woutT", woutT[:], SP, em.cnt[SP])
        dbg("ffnT", ffnT[:], SP, em.cnt[SP])
        dbg("wxT", wxT[:], SP, em.cnt[SP])
        dbg("wdtT", wdtT[0:64, :], SP, em.cnt[SP])

        mm_busy = [ffn_mm0_free, ffn_mm1_free]
        out_busy = [0, 0]
        psout_free = 0
        vp_res_prev = vp_prep_end
        tp_endI_prev = em.cnt[TP]
        vp_ypost_prev = 0
        vp_scan_a = {0: 0, 1: 0}
        vp_scan_b = {0: 0, 1: 0}
        vp_scan_B = {0: 0, 1: 0}
        vp_scan_C = {0: 0, 1: 0}
        bc_evac_sp = 0
        bc_evac_hist = {}
        gp_hfree = {0: 0, 1: 0}
        gp_scan_end = 0
        tp_trC = {0: 0, 1: 0}

        for c in range(NCH):
            t0 = c * TC
            em.wait("sync", VP, vp_res_prev)
            em.wait("sync", SP, stg_state["conv_sp"])
            ld_x = []
            for i in range(NTT):
                dma("sync", scrx[:, i * DM:(i + 1) * DM], x_h[t0 + i * 128: t0 + (i + 1) * 128, :])
                ld_x.append(em.cnt[LD])

            for i in range(NTT):
                em.wait("scalar", LD, ld_x[i])
                em.wait("scalar", TP, tp_trC[0])
                em.wait("scalar", TP, tp_trC[1])
                em.op("scalar", (lambda i=i: (lambda e: e.activation(
                    out=xn_tm[:], in_=scrx[:, i * DM:(i + 1) * DM], func=AF.Square,
                    accum_out=lnstat[:, i * 8 + 1: i * 8 + 2])))(), inc=SP)
            settle("scalar", SP)
            sp_sq = em.cnt[SP]
            em.wait("vector", SP, sp_sq)
            for i in range(NTT):
                em.op("vector", (lambda i=i: (lambda e: e.tensor_reduce(
                    out=lnstat[:, i * 8: i * 8 + 1], in_=scrx[:, i * DM:(i + 1) * DM],
                    axis=AX.X, op=ALU.add)))(), inc=VP)
            for i in range(NTT):
                em.op("vector", (lambda i=i: (lambda e: e.tensor_scalar_mul(
                    out=lnstat[:, i * 8 + 4: i * 8 + 5], in0=lnstat[:, i * 8 + 1: i * 8 + 2],
                    scalar1=1.0 / DM)))(), inc=VP)
            for i in range(NTT):
                em.op("vector", (lambda i=i: (lambda e: e.tensor_scalar_mul(
                    out=lnstat[:, i * 8 + 2: i * 8 + 3], in0=lnstat[:, i * 8: i * 8 + 1],
                    scalar1=-1.0 / DM)))(), inc=VP)
            settle("vector", VP)
            vp_nm = em.cnt[VP]
            em.wait("scalar", VP, vp_nm)
            for i in range(NTT):
                em.op("scalar", (lambda i=i: (lambda e: e.activation(
                    out=lnstat[:, i * 8 + 3: i * 8 + 4], in_=lnstat[:, i * 8 + 2: i * 8 + 3],
                    func=AF.Square)))(), inc=SP)
            settle("scalar", SP)
            sp_m2 = em.cnt[SP]
            em.wait("vector", SP, sp_m2)
            for i in range(NTT):
                em.op("vector", (lambda i=i: (lambda e: e.tensor_sub(
                    out=lnstat[:, i * 8 + 5: i * 8 + 6], in0=lnstat[:, i * 8 + 4: i * 8 + 5],
                    in1=lnstat[:, i * 8 + 3: i * 8 + 4])))(), inc=VP)
            settle("vector", VP)
            vp_var = em.cnt[VP]
            em.wait("scalar", VP, vp_var)
            for i in range(NTT):
                em.op("scalar", (lambda i=i: (lambda e: e.activation(
                    out=lnstat[:, i * 8 + 6: i * 8 + 7], in_=lnstat[:, i * 8 + 5: i * 8 + 6],
                    func=AF.Ln, bias=np.float32(EPS_LN))))(), inc=SP)
            for i in range(NTT):
                em.op("scalar", (lambda i=i: (lambda e: e.activation(
                    out=lnstat[:, i * 8 + 7: i * 8 + 8], in_=lnstat[:, i * 8 + 6: i * 8 + 7],
                    func=AF.Exp, scale=np.float32(-0.5))))(), inc=SP)
            settle("scalar", SP)
            sp_rstd = em.cnt[SP]
            em.wait("vector", SP, sp_rstd)
            for i in range(NTT):
                em.op("vector", (lambda i=i: (lambda e: e.tensor_mul(
                    out=lnstat[:, i * 8 + 5: i * 8 + 6], in0=lnstat[:, i * 8 + 2: i * 8 + 3],
                    in1=lnstat[:, i * 8 + 7: i * 8 + 8])))(), inc=VP)
            for i in range(NTT):
                par = i % 2
                em.wait("vector", TP, em.cnt[TP])
                em.op("vector", (lambda i=i: (lambda e: e.tensor_scalar(
                    out=xn_tm[:], in0=scrx[:, i * DM:(i + 1) * DM],
                    scalar1=lnstat[:, i * 8 + 7: i * 8 + 8],
                    scalar2=lnstat[:, i * 8 + 5: i * 8 + 6],
                    op0=ALU.mult, op1=ALU.add)))(), inc=VP)
                vp_xn = em.cnt[VP]
                em.wait("tensor", VP, vp_xn)
                em.wait("tensor", SP, tr_busy[par])
                for k in range(NKD):
                    em.op("tensor", (lambda k=k, p=par: (lambda e: e.transpose(
                        out=ps_tr[p][:, k * 128:(k + 1) * 128], in_=xn_tm[:, k * 128:(k + 1) * 128],
                        identity=identb[:])))(), inc=(TP if k == NKD - 1 else None))
                tp_trC[par] = em.cnt[TP]
                em.wait("scalar", TP, tp_trC[par])
                em.wait("scalar", TP, tp_endI_prev)
                em.op("scalar", (lambda p=par, i=i: (lambda e: e.activation(
                    out=xn_fm[:].rearrange("p (k t) -> p k t", k=NKD)[:, :, i * 128:(i + 1) * 128],
                    in_=ps_tr[p][:].rearrange("p (k t) -> p k t", k=NKD),
                    func=AF.Copy)))(), inc=SP)
                tr_busy[par] = em.cnt[SP]
            sp_xn = em.cnt[SP]
            if c == 0:
                dbg("lnstat", lnstat[:], VP, em.cnt[VP])
                dbg("scrx", scrx[:], VP, em.cnt[VP])
                dbg("xn_fm", xn_fm[:], SP, sp_xn)

            xp_evac_sp = {}
            em.wait("tensor", SP, sp_xn)
            em.wait("scalar", VP, vp_ypost_prev)

            def emit_conv(d):
                base = d * (TC + KC)
                cp = d % 2
                em.wait("vector", SP, em.cnt[SP])
                em.op("vector", (lambda base=base, d=d, cp=cp: (lambda e: e.tensor_scalar(
                    out=convacc[:, cp * TC:(cp + 1) * TC], in0=xpbuf[:, base + 3: base + 3 + TC],
                    scalar1=conw[:, d * KC + 3: d * KC + 4], scalar2=None, op0=ALU.mult)))(), inc=VP)
                for j in (2, 1, 0):
                    em.op("vector", (lambda base=base, d=d, j=j, cp=cp: (lambda e: e.scalar_tensor_tensor(
                        out=convacc[:, cp * TC:(cp + 1) * TC], in0=xpbuf[:, base + j: base + j + TC],
                        scalar=conw[:, d * KC + j: d * KC + j + 1],
                        in1=convacc[:, cp * TC:(cp + 1) * TC], op0=ALU.mult, op1=ALU.add)))(), inc=VP)
                em.op("vector", (lambda base=base: (lambda e: e.tensor_copy(
                    out=xpbuf[:, base: base + 3], in_=xpbuf[:, base + TC: base + TC + 3])))(), inc=VP)
                vp_conv = em.cnt[VP]
                em.wait("scalar", VP, vp_conv)
                em.op("scalar", (lambda d=d, cp=cp: (lambda e: e.activation(
                    out=u_sb[:, d * TC:(d + 1) * TC], in_=convacc[:, cp * TC:(cp + 1) * TC],
                    func=AF.Silu, bias=conb[:, d:d + 1])))(), inc=SP)

            for m in range(NMI):
                par = m % 2
                em.wait("tensor", SP, mm_busy[par])
                for k in range(NKD):
                    em.op("tensor", (lambda m=m, k=k, p=par: (lambda e: e.matmul(
                        out=ps_mm[p][:, 0:TC],
                        lhsT=winT[:, k * (2 * DI) + m * 128: k * (2 * DI) + (m + 1) * 128],
                        rhs=xn_fm[:, k * TC:(k + 1) * TC],
                        start=(k == 0), stop=(k == NKD - 1))))(),
                        inc=(TP if k == NKD - 1 else None))
                tpv = em.cnt[TP]
                em.wait("scalar", TP, tpv)
                if m < NDB:
                    em.op("scalar", (lambda m=m, p=par: (lambda e: e.activation(
                        out=xpbuf[:, m * (TC + KC) + 3: m * (TC + KC) + 3 + TC],
                        in_=ps_mm[p][:, 0:TC], func=AF.Copy)))(), inc=SP)
                else:
                    em.op("scalar", (lambda m=m, p=par: (lambda e: e.activation(
                        out=z_sb[:, (m - NDB) * TC:(m - NDB + 1) * TC],
                        in_=ps_mm[p][:, 0:TC], func=AF.Silu)))(), inc=SP)
                mm_busy[par] = em.cnt[SP]
                if 0 <= m - 2 < NDB:
                    emit_conv(m - 2)
            sp_u = em.cnt[SP]
            if c == 0:
                dbg("u", u_sb[:], SP, sp_u)

            em.wait("tensor", SP, sp_u)
            for k in range(NDB):
                em.op("tensor", (lambda k=k: (lambda e: e.matmul(
                    out=ps_mm[0][0:64, 0:TC], lhsT=wxT[:, k * 96: k * 96 + 64],
                    rhs=u_sb[:, k * TC:(k + 1) * TC], start=(k == 0), stop=(k == NDB - 1))))(),
                    inc=(TP if k == NDB - 1 else None))
            for k in range(NDB):
                em.op("tensor", (lambda k=k: (lambda e: e.matmul(
                    out=ps_mm[1][0:32, 0:TC], lhsT=wxT[:, k * 96 + 64: k * 96 + 96],
                    rhs=u_sb[:, k * TC:(k + 1) * TC], start=(k == 0), stop=(k == NDB - 1))))(),
                    inc=(TP if k == NDB - 1 else None))
            tpv = em.cnt[TP]
            em.wait("scalar", TP, tpv)
            em.op("scalar", lambda e: e.activation(out=dlt_sb[0:64, 0:TC], in_=ps_mm[0][0:64, 0:TC], func=AF.Copy), inc=SP)
            em.op("scalar", lambda e: e.activation(out=bc_sb[0:32, 0:TC], in_=ps_mm[1][0:32, 0:TC], func=AF.Copy), inc=SP)
            mm_busy[0] = mm_busy[1] = em.cnt[SP]
            sp_bc = em.cnt[SP]
            if c == 0:
                dbg("dlt", dlt_sb[0:64, :], SP, sp_bc)
                dbg("bc", bc_sb[0:32, :], SP, sp_bc)

            em.wait("tensor", SP, sp_bc)
            for d in range(NDB):
                par = d % 2
                em.wait("tensor", SP, mm_busy[par])
                em.op("tensor", (lambda d=d, p=par: (lambda e: e.matmul(
                    out=ps_mm[p][:, 0:TC], lhsT=wdtT[0:64, d * 128:(d + 1) * 128],
                    rhs=dlt_sb[0:64, 0:TC], start=True, stop=True)))(), inc=TP)
                tpv = em.cnt[TP]
                em.wait("scalar", TP, tpv)
                em.op("scalar", (lambda d=d, p=par: (lambda e: e.activation(
                    out=e_scr[:, p * TC:(p + 1) * TC], in_=ps_mm[p][:, 0:TC],
                    func=AF.Exp, bias=bdt[:, d:d + 1])))(), inc=SP)
                mm_busy[par] = em.cnt[SP]
                em.op("scalar", (lambda d=d, p=par: (lambda e: e.activation(
                    out=delta[:, d * TC:(d + 1) * TC], in_=e_scr[:, p * TC:(p + 1) * TC],
                    func=AF.Ln, bias=np.float32(1.0))))(), inc=SP)
            sp_delta = em.cnt[SP]

            bc_banks = [(ps_bcB, ps_bcC), (ps_mm[0], ps_mm[1])]
            bc_free_vp = {}
            for p2 in range(NST // 2):
                rot = p2 % 2
                pb, pc = bc_banks[rot]
                em.wait("tensor", SP, bc_evac_sp if p2 < 2 else bc_evac_hist[p2 - 2])
                for h2 in range(2):
                    n = 2 * p2 + h2
                    em.op("tensor", (lambda n=n, pb=pb, h2=h2: (lambda e: e.matmul(
                        out=pb[:, h2 * TC:(h2 + 1) * TC], lhsT=selB[:, n * 128:(n + 1) * 128],
                        rhs=bc_sb[0:32, 0:TC], start=True, stop=True)))(), inc=None)
                    em.op("tensor", (lambda n=n, pc=pc, h2=h2: (lambda e: e.matmul(
                        out=pc[:, h2 * TC:(h2 + 1) * TC], lhsT=selC[:, n * 128:(n + 1) * 128],
                        rhs=bc_sb[0:32, 0:TC], start=True, stop=(n == NST - 1))))(),
                        inc=(TP if h2 == 1 else None))
                tpv = em.cnt[TP]
                em.wait("scalar", TP, tpv)
                em.wait("scalar", ST, max(out_busy))
                em.wait("scalar", GP, gp_scan_end)
                em.op("scalar", (lambda p2=p2, pb=pb: (lambda e: e.activation(
                    out=Ball[:, 2 * p2 * TC: (2 * p2 + 2) * TC], in_=pb[:, 0:2 * TC],
                    func=AF.Copy)))(), inc=SP)
                em.op("scalar", (lambda p2=p2, pc=pc: (lambda e: e.activation(
                    out=Call[:, 2 * p2 * TC: (2 * p2 + 2) * TC], in_=pc[:, 0:2 * TC],
                    func=AF.Copy)))(), inc=SP)
                bc_evac_hist[p2] = em.cnt[SP]
            bc_evac_sp = em.cnt[SP]
            mm_busy[0] = mm_busy[1] = em.cnt[SP]
            sp_bcast = em.cnt[SP]
            if c == 0:
                dbg("delta", delta[:], SP, sp_delta)

            em.wait("vector", SP, sp_bcast)
            em.wait("vector", GP, gp_scan_end)
            for d in range(NDB):
                em.wait("vector", SP, sp_delta)
                em.op("vector", (lambda d=d: (lambda e: e.tensor_mul(
                    out=du_sb[:], in0=delta[:, d * TC:(d + 1) * TC],
                    in1=u_sb[:, d * TC:(d + 1) * TC])))(), inc=VP)
                for n in range(NST):
                    par = n % 2
                    em.wait("scalar", VP, vp_scan_a[par])
                    em.op("scalar", (lambda d=d, n=n, p=par: (lambda e: e.activation(
                        out=a_sb[:, p * TC:(p + 1) * TC], in_=delta[:, d * TC:(d + 1) * TC],
                        func=AF.Exp, scale=aneg[:, d * NST + n: d * NST + n + 1])))(), inc=SP)
                    sp_a = em.cnt[SP]
                    em.op("vector", (lambda n=n, p=par: (lambda e: e.tensor_mul(
                        out=b_sb[:, p * TC:(p + 1) * TC], in0=du_sb[:],
                        in1=Ball[:, n * TC:(n + 1) * TC])))(), inc=VP)
                    em.wait("vector", SP, sp_a)
                    em.wait("vector", GP, gp_hfree[par])
                    em.op("vector", (lambda d=d, n=n, p=par: (lambda e: e.tensor_tensor_scan(
                        out=h_sb[:, p * TC:(p + 1) * TC], data0=a_sb[:, p * TC:(p + 1) * TC],
                        data1=b_sb[:, p * TC:(p + 1) * TC],
                        initial=carry[:, d * NST + n: d * NST + n + 1],
                        op0=ALU.mult, op1=ALU.add)))(), inc=VP)
                    vp_scan_a[par] = em.cnt[VP]
                    em.wait("gpsimd", VP, vp_scan_a[par])
                    em.op("gpsimd", (lambda n=n, p=par: (lambda e: e.tensor_mul(
                        out=g_sb[:, p * TC:(p + 1) * TC], in0=h_sb[:, p * TC:(p + 1) * TC],
                        in1=Call[:, n * TC:(n + 1) * TC])))(), inc=GP)
                    em.op("gpsimd", (lambda d=d, n=n, p=par: (lambda e: e.tensor_copy(
                        out=carry[:, d * NST + n: d * NST + n + 1],
                        in_=h_sb[:, p * TC + TC - 1: p * TC + TC])))(), inc=GP)
                    gp_hfree[par] = em.cnt[GP]
                    if par == 1:
                        em.wait("vector", GP, gp_hfree[par])
                        em.op("vector", lambda e: e.tensor_add(
                            out=yp_sb[:, 0:TC], in0=g_sb[:, 0:TC], in1=g_sb[:, TC:2 * TC]), inc=VP)
                        if n == 1:
                            em.op("vector", lambda e: e.tensor_copy(out=yacc[:], in_=yp_sb[:, 0:TC]), inc=VP)
                        else:
                            em.op("vector", lambda e: e.tensor_add(out=yacc[:], in0=yacc[:], in1=yp_sb[:, 0:TC]), inc=VP)
                em.op("vector", (lambda d=d: (lambda e: e.scalar_tensor_tensor(
                    out=y2_sb[:], in0=u_sb[:, d * TC:(d + 1) * TC], scalar=dpar[:, d:d + 1],
                    in1=yacc[:], op0=ALU.mult, op1=ALU.add)))(), inc=VP)
                em.op("vector", (lambda d=d: (lambda e: e.tensor_mul(
                    out=ypost[:, d * TC:(d + 1) * TC], in0=y2_sb[:],
                    in1=z_sb[:, d * TC:(d + 1) * TC])))(), inc=VP)
            vp_ypost = em.cnt[VP]
            if c == 0:
                dbg("ypost", ypost[:], VP, vp_ypost)
                dbg("carry", carry[:], VP, vp_ypost)
            vp_ypost_prev = vp_ypost

            em.wait("tensor", VP, vp_ypost)
            for i in range(NTT):
                par = i % 2
                em.wait("tensor", VP, psout_free)
                for f in range(2):
                    fs = slice(f * 512, (f + 1) * 512)
                    for k in range(NDB):
                        em.op("tensor", (lambda i=i, k=k, fs=fs, f=f: (lambda e: e.matmul(
                            out=ps_out[:, fs], lhsT=ypost[:, k * TC + i * 128: k * TC + (i + 1) * 128],
                            rhs=woutT[:, k * DM + f * 512: k * DM + (f + 1) * 512],
                            start=(k == 0), stop=False)))(), inc=None)
                    for k in range(NKD):
                        em.op("tensor", (lambda i=i, k=k, fs=fs, f=f: (lambda e: e.matmul(
                            out=ps_out[:, fs], lhsT=xn_fm[:, k * TC + i * 128: k * TC + (i + 1) * 128],
                            rhs=ffnT[:, k * DM + f * 512: k * DM + (f + 1) * 512],
                            start=False, stop=(k == NKD - 1))))(),
                            inc=(TP if k == NKD - 1 else None))
                tpv = em.cnt[TP]
                em.wait("vector", TP, tpv)
                em.wait("vector", ST, out_busy[par])
                em.op("vector", (lambda i=i, p=par: (lambda e: e.tensor_add(
                    out=out_sb[:, p * DM:(p + 1) * DM], in0=ps_out[:, 0:DM],
                    in1=scrx[:, i * DM:(i + 1) * DM])))(), inc=VP)
                psout_free = em.cnt[VP]
                em.wait("gpsimd", VP, psout_free)
                em.op("gpsimd", (lambda i=i, p=par, t0=t0: (lambda e: e.dma_start(
                    out=out_h[t0 + i * 128: t0 + (i + 1) * 128, :],
                    in_=out_sb[:, p * DM:(p + 1) * DM])))(), inc=ST, k=16)
                out_busy[par] = em.cnt[ST]
            vp_res_prev = em.cnt[VP]
            tp_endI_prev = em.cnt[TP]

        em.wait("gpsimd", ST)

        sems = {name: ctx.enter_context(nc.semaphore(name))
                for name in ("LD", "ST", "TP", "VP", "SP", "GP")}
        block = ctx.enter_context(nc.Block())

        def mk(ename):
            def f(eng):
                for item in em.ops[ename]:
                    if item[0] == "wait":
                        _, s, v = item
                        eng.wait_ge(sems[s], v)
                    else:
                        _, fn, inc, k = item
                        inst = fn(eng)
                        if inc is not None:
                            inst.then_inc(sems[inc], k)
            return f

        block.sync(mk("sync"))
        block.tensor(mk("tensor"))
        block.vector(mk("vector"))
        block.scalar(mk("scalar"))
        block.gpsimd(mk("gpsimd"))

    return nc


_CONSTS = None


def _consts():
    global _CONSTS
    if _CONSTS is None:
        selB = np.zeros((2 * NST, NST * 128), dtype=ml_dtypes.bfloat16)
        selC = np.zeros((2 * NST, NST * 128), dtype=ml_dtypes.bfloat16)
        for n in range(NST):
            selB[n, n * 128:(n + 1) * 128] = 1
            selC[NST + n, n * 128:(n + 1) * 128] = 1
        _CONSTS = (selB, selC)
    return _CONSTS


def kernel(L0_attn_x, L0_attn_W_in, L0_attn_conv_w, L0_attn_conv_b, L0_attn_W_x,
           L0_attn_W_dt, L0_attn_b_dt, L0_attn_A_log, L0_attn_D_param,
           L0_attn_W_out, L0_ffn_weight):
    x = np.ascontiguousarray(np.asarray(L0_attn_x, dtype=np.float32))
    selB, selC = _consts()
    shared = {
        "w_in": np.ascontiguousarray(np.asarray(L0_attn_W_in, np.float32)),
        "conv_w": np.ascontiguousarray(np.asarray(L0_attn_conv_w, np.float32).reshape(DI, KC)),
        "conv_b": np.ascontiguousarray(np.asarray(L0_attn_conv_b, np.float32).reshape(DI, 1)),
        "w_x": np.ascontiguousarray(np.asarray(L0_attn_W_x, np.float32)),
        "w_dt": np.ascontiguousarray(np.asarray(L0_attn_W_dt, np.float32)),
        "b_dt": np.ascontiguousarray(np.asarray(L0_attn_b_dt, np.float32).reshape(DI, 1)),
        "a_log": np.ascontiguousarray(np.asarray(L0_attn_A_log, np.float32)),
        "d_par": np.ascontiguousarray(np.asarray(L0_attn_D_param, np.float32).reshape(DI, 1)),
        "w_out": np.ascontiguousarray(np.asarray(L0_attn_W_out, np.float32)),
        "w_ffn": np.ascontiguousarray(np.asarray(L0_ffn_weight, np.float32)),
        "ident": np.eye(128, dtype=ml_dtypes.bfloat16),
        "selB": selB,
        "selC": selC,
        "onesf": np.ones((1, 128), np.float32),
        "onesc": np.ones((128, 1), np.float32),
    }
    in_maps = [dict(shared, x=np.ascontiguousarray(x[b])) for b in range(8)]
    nc = build_program()
    res = run_bass_kernel_spmd(nc, in_maps, core_ids=list(range(8)))
    out = np.stack([np.asarray(r["out"], np.float32) for r in res.results], axis=0)
    return out


if __name__ == "__main__":
    nc = build_program()
    print("built ok")


# version 14
# speedup vs baseline: 1.4347x; 1.4347x over previous
import numpy as np
import ml_dtypes
from contextlib import ExitStack

import concourse.bass as bass
import concourse.mybir as mybir
from concourse.bass_utils import run_bass_kernel_spmd

F32 = mybir.dt.float32
BF16 = mybir.dt.bfloat16
AF = mybir.ActivationFunctionType
ALU = mybir.AluOpType
AX = mybir.AxisListType

L, DM, DI, NST, DR, KC = 4096, 1024, 2048, 16, 64, 4
TC = 512
NCH = L // TC
NTT = TC // 128
NDB = DI // 128
NKD = DM // 128
NMI = 2 * DI // 128
EPS_LN = 1e-5
EPS_FFN = 1e-5
DBG = False

ENGS = ("sync", "tensor", "vector", "scalar", "gpsimd")


class Em:

    def __init__(self):
        self.ops = {e: [] for e in ENGS}
        self.cnt = {}
        self.last = {e: {} for e in ENGS}

    def sem(self, name):
        self.cnt[name] = 0
        return name

    def wait(self, eng, sem, val=None):
        v = self.cnt[sem] if val is None else val
        if v <= 0:
            return
        if self.last[eng].get(sem, 0) >= v:
            return
        self.last[eng][sem] = v
        self.ops[eng].append(("wait", sem, v))

    def op(self, eng, fn, inc=None, k=1):
        if inc is not None:
            self.cnt[inc] += k
        self.ops[eng].append(("op", fn, inc, k))


def build_program():
    nc = bass.Bass("TRN2", target_bir_lowering=False, debug=False)

    x_h = nc.dram_tensor("x", [L, DM], F32, kind="ExternalInput").ap()
    win_h = nc.dram_tensor("w_in", [2 * DI, DM], F32, kind="ExternalInput").ap()
    conw_h = nc.dram_tensor("conv_w", [DI, KC], F32, kind="ExternalInput").ap()
    conb_h = nc.dram_tensor("conv_b", [DI, 1], F32, kind="ExternalInput").ap()
    wx_h = nc.dram_tensor("w_x", [DR + 2 * NST, DI], F32, kind="ExternalInput").ap()
    wdt_h = nc.dram_tensor("w_dt", [DI, DR], F32, kind="ExternalInput").ap()
    bdt_h = nc.dram_tensor("b_dt", [DI, 1], F32, kind="ExternalInput").ap()
    alog_h = nc.dram_tensor("a_log", [DI, NST], F32, kind="ExternalInput").ap()
    dpar_h = nc.dram_tensor("d_par", [DI, 1], F32, kind="ExternalInput").ap()
    wout_h = nc.dram_tensor("w_out", [DM, DI], F32, kind="ExternalInput").ap()
    wffn_h = nc.dram_tensor("w_ffn", [DM, DM], F32, kind="ExternalInput").ap()
    ident_h = nc.dram_tensor("ident", [128, 128], BF16, kind="ExternalInput").ap()
    selB_h = nc.dram_tensor("selB", [2 * NST, NST * 128], BF16, kind="ExternalInput").ap()
    selC_h = nc.dram_tensor("selC", [2 * NST, NST * 128], BF16, kind="ExternalInput").ap()
    onesf_h = nc.dram_tensor("onesf", [1, 128], F32, kind="ExternalInput").ap()
    onesc_h = nc.dram_tensor("onesc", [128, 1], F32, kind="ExternalInput").ap()
    out_h = nc.dram_tensor("out", [L, DM], F32, kind="ExternalOutput").ap()
    winTd = nc.dram_tensor("winT_scratch", [NMI * 128, NKD * 128], BF16, kind="Internal").ap()

    em = Em()
    LD, ST, TP, VP, SP = (em.sem(s) for s in ("LD", "ST", "TP", "VP", "SP"))

    with ExitStack() as ctx:
        sb = lambda *a: ctx.enter_context(nc.sbuf_tensor(*a))
        psum = lambda *a: ctx.enter_context(nc.psum_tensor(*a))

        win_stg = sb("win_stg", [128, 4 * NKD * 128], BF16)
        woutT = sb("woutT", [128, NDB * DM], BF16)
        ffnT = sb("ffnT", [128, NKD * DM], BF16)
        wxT = sb("wxT", [128, NDB * 96], BF16)
        wdtT = sb("wdtT", [64, DI], BF16)
        conw = sb("conw", [128, NDB * KC], F32)
        conb = sb("conb", [128, NDB], F32)
        bdt = sb("bdt", [128, NDB], F32)
        dpar = sb("dpar", [128, NDB], F32)
        aneg = sb("aneg", [128, NDB * NST], F32)
        selB = sb("selB_s", [2 * NST, NST * 128], BF16)
        selC = sb("selC_s", [2 * NST, NST * 128], BF16)
        identb = sb("identb", [128, 128], BF16)
        onesf = sb("onesf_s", [1, 128], F32)
        onesc = sb("onesc_s", [128, 1], F32)
        sth = sb("sth", [128, 4], F32)
        s_t = sb("s_t", [1, 4], F32)
        absacc = sb("absacc", [128, NKD + 1], F32)

        scrx = sb("scrx", [128, NTT * DM], F32)
        lnstat = sb("lnstat", [128, 8 * NTT], F32)
        xn_tm = sb("xn_tm", [128, DM], BF16)
        xn_fm = sb("xn_fm", [128, NKD * TC], BF16)
        wstg_f = scrx
        wstg_b = xn_fm
        xpbuf = sb("xpbuf", [128, NDB * (TC + KC)], BF16)
        convacc = sb("convacc", [128, 2 * TC], F32)
        z_sb = sb("z_sb", [128, NDB * TC], BF16)

        def u_ap(d):
            base = d * (TC + KC) + 3
            return xpbuf[:, base: base + TC]
        delta = sb("delta", [128, NDB * TC], BF16)
        dlt_sb = sb("dlt_sb", [64, TC], BF16)
        bc_sb = sb("bc_sb", [32, TC], BF16)
        e_scr = sb("e_scr", [128, 2 * TC], F32)
        du_sb = sb("du_sb", [128, TC], BF16)
        a_sb = sb("a_sb", [128, 2 * TC], F32)
        b_sb = sb("b_sb", [128, 2 * TC], BF16)
        Ball = sb("Ball", [128, NST * TC], BF16)
        Call = sb("Call", [128, NST * TC], BF16)
        h_sb = sb("h_sb", [128, 2 * TC], BF16)
        g_sb = sb("g_sb", [128, 2 * TC], BF16)
        yp_sb = sb("yp_sb", [128, 2 * TC], BF16)
        yp2_sb = sb("yp2_sb", [128, TC], BF16)
        yacc = sb("yacc", [128, TC], F32)
        y2_sb = sb("y2_sb", [128, TC], BF16)
        carry = sb("carry", [128, NDB * NST], F32)
        ypost = delta
        out_sb = Ball[:].bitcast(F32)

        ps_mm = [psum("ps_mmA", [128, 512], F32), psum("ps_mmB", [128, 512], F32)]
        ps_bcB = psum("ps_bcB", [128, 512], F32)
        ps_bcC = psum("ps_bcC", [128, 512], F32)
        ps_out = psum("ps_out", [128, 1024], F32)
        ps_tr = [psum("ps_trA", [128, 1024], BF16), psum("ps_trB", [128, 1024], BF16)]

        def dma(eng, out_ap, in_ap, sem=LD):
            em.op(eng, (lambda o=out_ap, i=in_ap: (lambda e: e.dma_start(out=o, in_=i)))(), inc=sem, k=16)

        dbg_handles = {}

        vdum = sb("vdum", [128, 4], F32)

        def settle(eng, sem):
            for _ in range(2):
                if eng == "vector":
                    em.op("vector", lambda e: e.memset(vdum[:, 0:2], 0.0), inc=sem)
                elif eng == "scalar":
                    em.op("scalar", lambda e: e.activation(out=vdum[:, 2:3], in_=vdum[:, 3:4], func=AF.Copy), inc=sem)

        def dbg(name, src_ap, psem, pval):
            if not DBG:
                return
            dt = src_ap.dtype
            shp = list(src_ap.shape)
            h = nc.dram_tensor("dbg_" + name, shp, dt, kind="ExternalOutput").ap()
            dbg_handles[name] = h
            em.wait("gpsimd", psem, pval)
            dma("gpsimd", h, src_ap, sem=ST)
            stv = em.cnt[ST]
            for e in ("sync", "tensor", "vector", "scalar"):
                em.wait(e, ST, stv)

        for d in range(NDB):
            r = slice(d * 128, (d + 1) * 128)
            dma("sync", conw[:, d * KC:(d + 1) * KC], conw_h[r, :])
            dma("sync", conb[:, d:d + 1], conb_h[r, :])
            dma("sync", bdt[:, d:d + 1], bdt_h[r, :])
            dma("sync", dpar[:, d:d + 1], dpar_h[r, :])
            dma("sync", aneg[:, d * NST:(d + 1) * NST], alog_h[r, :])
        dma("sync", identb[:], ident_h[:])
        dma("sync", selB[:], selB_h[:])
        dma("sync", selC[:], selC_h[:])
        dma("sync", onesf[:], onesf_h[:])
        dma("sync", onesc[:], onesc_h[:])
        ld_consts = em.cnt[LD]

        em.wait("scalar", LD, ld_consts)
        em.op("scalar", lambda e: e.activation(out=aneg[:], in_=aneg[:], func=AF.Exp), inc=SP)
        em.wait("vector", SP)
        em.op("vector", lambda e: e.tensor_scalar_mul(out=aneg[:], in0=aneg[:], scalar1=-1.0), inc=VP)
        em.op("vector", lambda e: e.memset(carry[:], 0.0), inc=VP)
        em.op("vector", lambda e: e.memset(xpbuf[:], 0.0), inc=VP)

        tr_par = [0]
        tr_busy = {0: 0, 1: 0}
        stg_state = {"conv_sp": 0, "tr_tp": 0}

        def prep_block(src_ap, ncols, nkb, evac_fn, stg_off):
            rows = src_ap.shape[0]
            fcols = slice(stg_off, stg_off + ncols)
            em.wait("sync", SP, stg_state["conv_sp"])
            dma("sync", wstg_f[0:rows, fcols], src_ap)
            ldv = em.cnt[LD]
            em.wait("scalar", LD, ldv)
            em.wait("scalar", TP, stg_state["tr_tp"])
            em.op("scalar", (lambda r=rows, fc=fcols: (lambda e: e.activation(
                out=wstg_b[0:r, fc], in_=wstg_f[0:r, fc], func=AF.Copy)))(), inc=SP)
            stg_state["conv_sp"] = em.cnt[SP]
            em.wait("tensor", SP, stg_state["conv_sp"])
            for kb in range(nkb):
                par = tr_par[0]
                tr_par[0] ^= 1
                em.wait("tensor", SP, tr_busy[par])
                c0 = stg_off + kb * 128
                w = min(ncols - kb * 128, 128)
                em.op("tensor", (lambda r=rows, c0=c0, p=par, w=w: (lambda e: e.transpose(
                    out=ps_tr[p][0:w, 0:r], in_=wstg_b[0:r, c0:c0 + w],
                    identity=identb[0:r, 0:r])))(), inc=TP)
                stg_state["tr_tp"] = em.cnt[TP]
                em.wait("scalar", TP, stg_state["tr_tp"])
                out_ap, hh, ww = evac_fn(kb)
                em.op("scalar", (lambda o=out_ap, p=par, hh=hh, ww=ww: (lambda e: e.activation(
                    out=o, in_=ps_tr[p][0:hh, 0:ww], func=AF.Copy)))(), inc=SP)
                tr_busy[par] = em.cnt[SP]

        st_win = {}
        for j in range(NMI):
            if j >= 4:
                em.wait("scalar", ST, st_win[j - 4])
            prep_block(win_h[j * 128:(j + 1) * 128, :], DM, NKD,
                       (lambda j=j: lambda kb: (
                           win_stg[:, (j % 4) * 1024 + kb * 128: (j % 4) * 1024 + (kb + 1) * 128], 128, 128))(),
                       (j % 2) * 1024)
            em.wait("sync", SP)
            dma("sync", winTd[j * 128:(j + 1) * 128, :],
                win_stg[:, (j % 4) * 1024:(j % 4 + 1) * 1024], sem=ST)
            st_win[j] = em.cnt[ST]
        for j in range(NKD):
            prep_block(wout_h[j * 128:(j + 1) * 128, :], DI, NDB,
                       (lambda j=j: lambda kb: (
                           woutT[:, kb * DM + j * 128: kb * DM + (j + 1) * 128], 128, 128))(),
                       0)
        prep_block(wx_h[:, :], DI, NDB,
                   lambda kb: (wxT[:, kb * 96:(kb + 1) * 96], 128, 96), 0)
        for j in range(NDB):
            prep_block(wdt_h[j * 128:(j + 1) * 128, :], DR, 1,
                       (lambda j=j: lambda kb: (wdtT[0:64, j * 128:(j + 1) * 128], 64, 128))(),
                       (j % 2) * 1024)

        for j in range(NKD):
            fc = slice((j % 2) * 1024, (j % 2) * 1024 + 1024)
            em.wait("sync", SP, stg_state["conv_sp"])
            em.wait("sync", VP, stg_state.get("ffn_vp", 0))
            dma("sync", wstg_f[:, fc], wffn_h[j * 128:(j + 1) * 128, :])
            ldv = em.cnt[LD]
            em.wait("vector", LD, ldv)
            em.op("vector", lambda e: e.nop(cycle_cnt=10000), inc=VP)
            em.op("vector", (lambda j=j, fc=fc: (lambda e: e.tensor_reduce(
                out=absacc[:, j:j + 1], in_=wstg_f[:, fc],
                axis=AX.X, op=ALU.add, apply_absolute_value=True)))(), inc=VP)
            if j % 2 == 1:
                stg_state["ffn_vp"] = em.cnt[VP]
        settle("vector", VP)
        em.op("vector", lambda e: e.tensor_reduce(
            out=absacc[:, NKD:NKD + 1], in_=absacc[:, 0:NKD], axis=AX.X, op=ALU.add), inc=VP)
        settle("vector", VP)
        em.wait("tensor", VP)
        em.wait("tensor", SP, em.cnt[SP])
        em.op("tensor", lambda e: e.matmul(out=ps_mm[0][0:1, 0:1], lhsT=absacc[:, NKD:NKD + 1],
                                           rhs=onesc[:, 0:1], start=True, stop=True), inc=TP)
        em.wait("scalar", TP)
        em.op("scalar", lambda e: e.activation(out=s_t[0:1, 0:1], in_=ps_mm[0][0:1, 0:1], func=AF.Copy), inc=SP)
        ffn_mm0_free = em.cnt[SP]
        settle("scalar", SP)
        em.wait("vector", SP)
        em.op("vector", lambda e: e.tensor_scalar(
            out=s_t[0:1, 2:3], in0=s_t[0:1, 0:1], scalar1=-0.5 / (DM * DM),
            scalar2=-0.5 * EPS_FFN, op0=ALU.mult, op1=ALU.min), inc=VP)
        settle("vector", VP)
        em.wait("tensor", VP)
        em.op("tensor", lambda e: e.matmul(out=ps_mm[1][:, 0:1], lhsT=onesf[0:1, :],
                                           rhs=s_t[0:1, 2:3], start=True, stop=True), inc=TP)
        em.wait("scalar", TP)
        em.op("scalar", lambda e: e.activation(out=sth[:, 0:1], in_=ps_mm[1][:, 0:1], func=AF.Copy), inc=SP)
        ffn_mm1_free = em.cnt[SP]
        settle("scalar", SP)
        em.wait("vector", SP)
        em.op("vector", lambda e: e.tensor_scalar_mul(out=sth[:, 1:2], in0=sth[:, 0:1], scalar1=-1.0), inc=VP)
        em.op("vector", lambda e: e.tensor_scalar_mul(out=sth[:, 2:3], in0=sth[:, 0:1], scalar1=2.0), inc=VP)
        settle("vector", VP)
        vp_sth = em.cnt[VP]

        for j in range(NKD):
            fc = slice((j % 2) * 1024, (j % 2) * 1024 + 1024)
            em.wait("sync", SP, stg_state["conv_sp"])
            em.wait("sync", VP, stg_state.get("ffnB_vp", vp_sth))
            dma("sync", wstg_f[:, fc], wffn_h[j * 128:(j + 1) * 128, :])
            ldv = em.cnt[LD]
            em.wait("vector", LD, ldv)
            em.wait("vector", VP, vp_sth)
            em.op("vector", lambda e: e.nop(cycle_cnt=10000), inc=VP)
            em.op("vector", (lambda fc=fc: (lambda e: e.tensor_scalar(
                out=z_sb[:, 0:1024], in0=wstg_f[:, fc], scalar1=sth[:, 1:2], scalar2=None,
                op0=ALU.is_gt)))(), inc=VP)
            em.op("vector", (lambda fc=fc: (lambda e: e.scalar_tensor_tensor(
                out=z_sb[:, 1024:2048], in0=wstg_f[:, fc], scalar=sth[:, 0:1],
                in1=z_sb[:, 0:1024], op0=ALU.is_lt, op1=ALU.subtract)))(), inc=VP)
            if j % 2 == 1:
                stg_state["ffnB_vp"] = em.cnt[VP]
            em.wait("vector", TP, stg_state["tr_tp"])
            em.op("vector", (lambda fc=fc: (lambda e: e.tensor_scalar(
                out=wstg_b[:, fc], in0=z_sb[:, 1024:2048], scalar1=sth[:, 2:3], scalar2=None,
                op0=ALU.mult)))(), inc=VP)
            vpv = em.cnt[VP]
            em.wait("tensor", VP, vpv)
            for kb in range(NKD):
                par = tr_par[0]
                tr_par[0] ^= 1
                em.wait("tensor", SP, tr_busy[par])
                em.op("tensor", (lambda kb=kb, p=par, fc=fc: (lambda e: e.transpose(
                    out=ps_tr[p][:, 0:128], in_=wstg_b[:, fc.start + kb * 128: fc.start + (kb + 1) * 128],
                    identity=identb[:])))(), inc=TP)
                stg_state["tr_tp"] = em.cnt[TP]
                em.wait("scalar", TP, stg_state["tr_tp"])
                em.op("scalar", (lambda kb=kb, j=j, p=par: (lambda e: e.activation(
                    out=ffnT[:, kb * DM + j * 128: kb * DM + (j + 1) * 128],
                    in_=ps_tr[p][:, 0:128], func=AF.Copy)))(), inc=SP)
                tr_busy[par] = em.cnt[SP]
        vp_prep_end = em.cnt[VP]

        dbg("absacc", absacc[:], VP, em.cnt[VP])
        dbg("sth", sth[:], VP, em.cnt[VP])
        dbg("woutT", woutT[:], SP, em.cnt[SP])
        dbg("ffnT", ffnT[:], SP, em.cnt[SP])
        dbg("wxT", wxT[:], SP, em.cnt[SP])
        dbg("wdtT", wdtT[0:64, :], SP, em.cnt[SP])

        mm_busy = [ffn_mm0_free, ffn_mm1_free]
        out_busy = [0, 0]
        psout_free = 0
        vp_res_prev = vp_prep_end
        tp_endI_prev = em.cnt[TP]
        vp_ypost_prev = 0
        vp_scan_a = {0: 0, 1: 0}
        bc_busy = {0: 0, 1: 0}
        tp_trC = {0: 0, 1: 0}

        for c in range(NCH):
            t0 = c * TC
            em.wait("sync", VP, vp_res_prev)
            em.wait("sync", SP, stg_state["conv_sp"])
            em.wait("sync", ST, max(st_win.values()))
            ld_x = []
            for i in range(NTT):
                dma("sync", scrx[:, i * DM:(i + 1) * DM], x_h[t0 + i * 128: t0 + (i + 1) * 128, :])
                ld_x.append(em.cnt[LD])

            for i in range(NTT):
                em.wait("scalar", LD, ld_x[i])
                em.wait("scalar", TP, tp_trC[0])
                em.wait("scalar", TP, tp_trC[1])
                em.op("scalar", (lambda i=i: (lambda e: e.activation(
                    out=xn_tm[:], in_=scrx[:, i * DM:(i + 1) * DM], func=AF.Square,
                    accum_out=lnstat[:, i * 8 + 1: i * 8 + 2])))(), inc=SP)
            settle("scalar", SP)
            sp_sq = em.cnt[SP]
            em.wait("vector", SP, sp_sq)
            for i in range(NTT):
                em.op("vector", (lambda i=i: (lambda e: e.tensor_reduce(
                    out=lnstat[:, i * 8: i * 8 + 1], in_=scrx[:, i * DM:(i + 1) * DM],
                    axis=AX.X, op=ALU.add)))(), inc=VP)
            for i in range(NTT):
                em.op("vector", (lambda i=i: (lambda e: e.tensor_scalar_mul(
                    out=lnstat[:, i * 8 + 4: i * 8 + 5], in0=lnstat[:, i * 8 + 1: i * 8 + 2],
                    scalar1=1.0 / DM)))(), inc=VP)
            for i in range(NTT):
                em.op("vector", (lambda i=i: (lambda e: e.tensor_scalar_mul(
                    out=lnstat[:, i * 8 + 2: i * 8 + 3], in0=lnstat[:, i * 8: i * 8 + 1],
                    scalar1=-1.0 / DM)))(), inc=VP)
            settle("vector", VP)
            vp_nm = em.cnt[VP]
            em.wait("scalar", VP, vp_nm)
            for i in range(NTT):
                em.op("scalar", (lambda i=i: (lambda e: e.activation(
                    out=lnstat[:, i * 8 + 3: i * 8 + 4], in_=lnstat[:, i * 8 + 2: i * 8 + 3],
                    func=AF.Square)))(), inc=SP)
            settle("scalar", SP)
            sp_m2 = em.cnt[SP]
            em.wait("vector", SP, sp_m2)
            for i in range(NTT):
                em.op("vector", (lambda i=i: (lambda e: e.tensor_sub(
                    out=lnstat[:, i * 8 + 5: i * 8 + 6], in0=lnstat[:, i * 8 + 4: i * 8 + 5],
                    in1=lnstat[:, i * 8 + 3: i * 8 + 4])))(), inc=VP)
            settle("vector", VP)
            vp_var = em.cnt[VP]
            em.wait("scalar", VP, vp_var)
            for i in range(NTT):
                em.op("scalar", (lambda i=i: (lambda e: e.activation(
                    out=lnstat[:, i * 8 + 6: i * 8 + 7], in_=lnstat[:, i * 8 + 5: i * 8 + 6],
                    func=AF.Ln, bias=np.float32(EPS_LN))))(), inc=SP)
            for i in range(NTT):
                em.op("scalar", (lambda i=i: (lambda e: e.activation(
                    out=lnstat[:, i * 8 + 7: i * 8 + 8], in_=lnstat[:, i * 8 + 6: i * 8 + 7],
                    func=AF.Exp, scale=np.float32(-0.5))))(), inc=SP)
            settle("scalar", SP)
            sp_rstd = em.cnt[SP]
            em.wait("vector", SP, sp_rstd)
            for i in range(NTT):
                em.op("vector", (lambda i=i: (lambda e: e.tensor_mul(
                    out=lnstat[:, i * 8 + 5: i * 8 + 6], in0=lnstat[:, i * 8 + 2: i * 8 + 3],
                    in1=lnstat[:, i * 8 + 7: i * 8 + 8])))(), inc=VP)
            for i in range(NTT):
                par = i % 2
                em.wait("vector", TP, em.cnt[TP])
                em.op("vector", (lambda i=i: (lambda e: e.tensor_scalar(
                    out=xn_tm[:], in0=scrx[:, i * DM:(i + 1) * DM],
                    scalar1=lnstat[:, i * 8 + 7: i * 8 + 8],
                    scalar2=lnstat[:, i * 8 + 5: i * 8 + 6],
                    op0=ALU.mult, op1=ALU.add)))(), inc=VP)
                vp_xn = em.cnt[VP]
                em.wait("tensor", VP, vp_xn)
                em.wait("tensor", SP, tr_busy[par])
                for k in range(NKD):
                    em.op("tensor", (lambda k=k, p=par: (lambda e: e.transpose(
                        out=ps_tr[p][:, k * 128:(k + 1) * 128], in_=xn_tm[:, k * 128:(k + 1) * 128],
                        identity=identb[:])))(), inc=(TP if k == NKD - 1 else None))
                tp_trC[par] = em.cnt[TP]
                em.wait("scalar", TP, tp_trC[par])
                em.wait("scalar", TP, tp_endI_prev)
                em.op("scalar", (lambda p=par, i=i: (lambda e: e.activation(
                    out=xn_fm[:].rearrange("p (k t) -> p k t", k=NKD)[:, :, i * 128:(i + 1) * 128],
                    in_=ps_tr[p][:].rearrange("p (k t) -> p k t", k=NKD),
                    func=AF.Copy)))(), inc=SP)
                tr_busy[par] = em.cnt[SP]
            sp_xn = em.cnt[SP]
            if c == 0:
                dbg("lnstat", lnstat[:], VP, em.cnt[VP])
                dbg("scrx", scrx[:], VP, em.cnt[VP])
                dbg("xn_fm", xn_fm[:], SP, sp_xn)

            xp_evac_sp = {}
            em.wait("tensor", SP, sp_xn)
            em.wait("scalar", VP, vp_ypost_prev)

            def emit_conv(d):
                base = d * (TC + KC)
                cp = d % 2
                em.wait("vector", SP, em.cnt[SP])
                em.op("vector", (lambda base=base, d=d, cp=cp: (lambda e: e.tensor_scalar(
                    out=convacc[:, cp * TC:(cp + 1) * TC], in0=xpbuf[:, base + 3: base + 3 + TC],
                    scalar1=conw[:, d * KC + 3: d * KC + 4], scalar2=None, op0=ALU.mult)))(), inc=VP)
                for j in (2, 1, 0):
                    em.op("vector", (lambda base=base, d=d, j=j, cp=cp: (lambda e: e.scalar_tensor_tensor(
                        out=convacc[:, cp * TC:(cp + 1) * TC], in0=xpbuf[:, base + j: base + j + TC],
                        scalar=conw[:, d * KC + j: d * KC + j + 1],
                        in1=convacc[:, cp * TC:(cp + 1) * TC], op0=ALU.mult, op1=ALU.add)))(), inc=VP)
                em.op("vector", (lambda base=base: (lambda e: e.tensor_copy(
                    out=xpbuf[:, base: base + 3], in_=xpbuf[:, base + TC: base + TC + 3])))(), inc=VP)
                vp_conv = em.cnt[VP]
                em.wait("scalar", VP, vp_conv)
                em.op("scalar", (lambda d=d, cp=cp: (lambda e: e.activation(
                    out=u_ap(d), in_=convacc[:, cp * TC:(cp + 1) * TC],
                    func=AF.Silu, bias=conb[:, d:d + 1])))(), inc=SP)

            ld_win = {}
            for m in range(4):
                dma("sync", win_stg[:, (m % 4) * 1024:(m % 4 + 1) * 1024],
                    winTd[m * 128:(m + 1) * 128, :])
                ld_win[m] = em.cnt[LD]
            for m in range(NMI):
                par = m % 2
                s4 = m % 4
                em.wait("tensor", SP, mm_busy[par])
                em.wait("tensor", LD, ld_win[m])
                for k in range(NKD):
                    em.op("tensor", (lambda m=m, k=k, p=par, s4=s4: (lambda e: e.matmul(
                        out=ps_mm[p][:, 0:TC],
                        lhsT=win_stg[:, s4 * 1024 + k * 128: s4 * 1024 + (k + 1) * 128],
                        rhs=xn_fm[:, k * TC:(k + 1) * TC],
                        start=(k == 0), stop=(k == NKD - 1))))(),
                        inc=(TP if k == NKD - 1 else None))
                tpv = em.cnt[TP]
                if m + 4 < NMI:
                    em.wait("sync", TP, tpv)
                    dma("sync", win_stg[:, s4 * 1024:(s4 + 1) * 1024],
                        winTd[(m + 4) * 128:(m + 5) * 128, :])
                    ld_win[m + 4] = em.cnt[LD]
                em.wait("scalar", TP, tpv)
                if m < NDB:
                    em.op("scalar", (lambda m=m, p=par: (lambda e: e.activation(
                        out=xpbuf[:, m * (TC + KC) + 3: m * (TC + KC) + 3 + TC],
                        in_=ps_mm[p][:, 0:TC], func=AF.Copy)))(), inc=SP)
                else:
                    em.op("scalar", (lambda m=m, p=par: (lambda e: e.activation(
                        out=z_sb[:, (m - NDB) * TC:(m - NDB + 1) * TC],
                        in_=ps_mm[p][:, 0:TC], func=AF.Silu)))(), inc=SP)
                mm_busy[par] = em.cnt[SP]
                if 0 <= m - 2 < NDB:
                    emit_conv(m - 2)
            sp_u = em.cnt[SP]
            if c == 0:
                dbg("u", xpbuf[:], SP, sp_u)

            em.wait("tensor", SP, sp_u)
            for k in range(NDB):
                em.op("tensor", (lambda k=k: (lambda e: e.matmul(
                    out=ps_mm[0][0:64, 0:TC], lhsT=wxT[:, k * 96: k * 96 + 64],
                    rhs=u_ap(k), start=(k == 0), stop=(k == NDB - 1))))(),
                    inc=(TP if k == NDB - 1 else None))
            for k in range(NDB):
                em.op("tensor", (lambda k=k: (lambda e: e.matmul(
                    out=ps_mm[1][0:32, 0:TC], lhsT=wxT[:, k * 96 + 64: k * 96 + 96],
                    rhs=u_ap(k), start=(k == 0), stop=(k == NDB - 1))))(),
                    inc=(TP if k == NDB - 1 else None))
            tpv = em.cnt[TP]
            em.wait("scalar", TP, tpv)
            em.op("scalar", lambda e: e.activation(out=dlt_sb[0:64, 0:TC], in_=ps_mm[0][0:64, 0:TC], func=AF.Copy), inc=SP)
            em.op("scalar", lambda e: e.activation(out=bc_sb[0:32, 0:TC], in_=ps_mm[1][0:32, 0:TC], func=AF.Copy), inc=SP)
            mm_busy[0] = mm_busy[1] = em.cnt[SP]
            sp_bc = em.cnt[SP]
            if c == 0:
                dbg("dlt", dlt_sb[0:64, :], SP, sp_bc)
                dbg("bc", bc_sb[0:32, :], SP, sp_bc)

            em.wait("tensor", SP, sp_bc)
            for d in range(NDB):
                par = d % 2
                em.wait("tensor", SP, mm_busy[par])
                em.op("tensor", (lambda d=d, p=par: (lambda e: e.matmul(
                    out=ps_mm[p][:, 0:TC], lhsT=wdtT[0:64, d * 128:(d + 1) * 128],
                    rhs=dlt_sb[0:64, 0:TC], start=True, stop=True)))(), inc=TP)
                tpv = em.cnt[TP]
                em.wait("scalar", TP, tpv)
                em.op("scalar", (lambda d=d, p=par: (lambda e: e.activation(
                    out=e_scr[:, p * TC:(p + 1) * TC], in_=ps_mm[p][:, 0:TC],
                    func=AF.Exp, bias=bdt[:, d:d + 1])))(), inc=SP)
                mm_busy[par] = em.cnt[SP]
                em.op("scalar", (lambda d=d, p=par: (lambda e: e.activation(
                    out=delta[:, d * TC:(d + 1) * TC], in_=e_scr[:, p * TC:(p + 1) * TC],
                    func=AF.Ln, bias=np.float32(1.0))))(), inc=SP)
            sp_delta = em.cnt[SP]

            bank_list = [ps_bcB, ps_bcC, ps_mm[0], ps_mm[1]]
            for n in range(NST):
                ib = (2 * n) % 4
                ic = (2 * n + 1) % 4
                for bk in (ib, ic):
                    em.wait("tensor", SP, mm_busy[bk - 2] if bk >= 2 else bc_busy[bk])
                em.op("tensor", (lambda n=n, pb=bank_list[ib]: (lambda e: e.matmul(
                    out=pb[:, 0:TC], lhsT=selB[:, n * 128:(n + 1) * 128],
                    rhs=bc_sb[0:32, 0:TC], start=True, stop=True)))(), inc=None)
                em.op("tensor", (lambda n=n, pc=bank_list[ic]: (lambda e: e.matmul(
                    out=pc[:, 0:TC], lhsT=selC[:, n * 128:(n + 1) * 128],
                    rhs=bc_sb[0:32, 0:TC], start=True, stop=(n == NST - 1))))(), inc=TP)
                tpv = em.cnt[TP]
                em.wait("scalar", TP, tpv)
                if n == 0:
                    em.wait("scalar", ST, max(out_busy))
                em.op("scalar", (lambda n=n, pb=bank_list[ib]: (lambda e: e.activation(
                    out=Ball[:, n * TC:(n + 1) * TC], in_=pb[:, 0:TC],
                    func=AF.Copy)))(), inc=SP)
                em.op("scalar", (lambda n=n, pc=bank_list[ic]: (lambda e: e.activation(
                    out=Call[:, n * TC:(n + 1) * TC], in_=pc[:, 0:TC],
                    func=AF.Copy)))(), inc=SP)
                for bk in (ib, ic):
                    if bk >= 2:
                        mm_busy[bk - 2] = em.cnt[SP]
                    else:
                        bc_busy[bk] = em.cnt[SP]
            mm_busy[0] = mm_busy[1] = em.cnt[SP]
            sp_bcast = em.cnt[SP]
            if c == 0:
                dbg("delta", delta[:], SP, sp_delta)

            em.wait("vector", SP, sp_bcast)
            for d in range(NDB):
                em.wait("vector", SP, sp_delta)
                em.op("vector", (lambda d=d: (lambda e: e.tensor_mul(
                    out=du_sb[:], in0=delta[:, d * TC:(d + 1) * TC],
                    in1=u_ap(d))))(), inc=VP)
                for n in range(NST):
                    par = n % 2
                    em.wait("scalar", VP, vp_scan_a[par])
                    em.op("scalar", (lambda d=d, n=n, p=par: (lambda e: e.activation(
                        out=a_sb[:, p * TC:(p + 1) * TC], in_=delta[:, d * TC:(d + 1) * TC],
                        func=AF.Exp, scale=aneg[:, d * NST + n: d * NST + n + 1])))(), inc=SP)
                    sp_a = em.cnt[SP]
                    em.op("vector", (lambda n=n, p=par: (lambda e: e.tensor_mul(
                        out=b_sb[:, p * TC:(p + 1) * TC], in0=du_sb[:],
                        in1=Ball[:, n * TC:(n + 1) * TC])))(), inc=VP)
                    em.wait("vector", SP, sp_a)
                    em.op("vector", (lambda d=d, n=n, p=par: (lambda e: e.tensor_tensor_scan(
                        out=h_sb[:, p * TC:(p + 1) * TC], data0=a_sb[:, p * TC:(p + 1) * TC],
                        data1=b_sb[:, p * TC:(p + 1) * TC],
                        initial=carry[:, d * NST + n: d * NST + n + 1],
                        op0=ALU.mult, op1=ALU.add)))(), inc=VP)
                    vp_scan_a[par] = em.cnt[VP]
                    em.op("vector", (lambda n=n, p=par: (lambda e: e.tensor_mul(
                        out=g_sb[:, p * TC:(p + 1) * TC], in0=h_sb[:, p * TC:(p + 1) * TC],
                        in1=Call[:, n * TC:(n + 1) * TC])))(), inc=VP)
                    em.op("vector", (lambda d=d, n=n, p=par: (lambda e: e.tensor_copy(
                        out=carry[:, d * NST + n: d * NST + n + 1],
                        in_=h_sb[:, p * TC + TC - 1: p * TC + TC])))(), inc=VP)
                    if par == 1:
                        em.op("vector", lambda e: e.tensor_add(
                            out=yp_sb[:, 0:TC], in0=g_sb[:, 0:TC], in1=g_sb[:, TC:2 * TC]), inc=VP)
                        if n == 1:
                            em.op("vector", lambda e: e.tensor_copy(out=yacc[:], in_=yp_sb[:, 0:TC]), inc=VP)
                        else:
                            em.op("vector", lambda e: e.tensor_add(out=yacc[:], in0=yacc[:], in1=yp_sb[:, 0:TC]), inc=VP)
                em.op("vector", (lambda d=d: (lambda e: e.scalar_tensor_tensor(
                    out=y2_sb[:], in0=u_ap(d), scalar=dpar[:, d:d + 1],
                    in1=yacc[:], op0=ALU.mult, op1=ALU.add)))(), inc=VP)
                em.op("vector", (lambda d=d: (lambda e: e.tensor_mul(
                    out=ypost[:, d * TC:(d + 1) * TC], in0=y2_sb[:],
                    in1=z_sb[:, d * TC:(d + 1) * TC])))(), inc=VP)
            vp_ypost = em.cnt[VP]
            if c == 0:
                dbg("ypost", ypost[:], VP, vp_ypost)
                dbg("carry", carry[:], VP, vp_ypost)
            vp_ypost_prev = vp_ypost

            em.wait("tensor", VP, vp_ypost)
            for i in range(NTT):
                par = i % 2
                em.wait("tensor", VP, psout_free)
                for f in range(2):
                    fs = slice(f * 512, (f + 1) * 512)
                    for k in range(NDB):
                        em.op("tensor", (lambda i=i, k=k, fs=fs, f=f: (lambda e: e.matmul(
                            out=ps_out[:, fs], lhsT=ypost[:, k * TC + i * 128: k * TC + (i + 1) * 128],
                            rhs=woutT[:, k * DM + f * 512: k * DM + (f + 1) * 512],
                            start=(k == 0), stop=False)))(), inc=None)
                    for k in range(NKD):
                        em.op("tensor", (lambda i=i, k=k, fs=fs, f=f: (lambda e: e.matmul(
                            out=ps_out[:, fs], lhsT=xn_fm[:, k * TC + i * 128: k * TC + (i + 1) * 128],
                            rhs=ffnT[:, k * DM + f * 512: k * DM + (f + 1) * 512],
                            start=False, stop=(k == NKD - 1))))(),
                            inc=(TP if k == NKD - 1 else None))
                tpv = em.cnt[TP]
                em.wait("vector", TP, tpv)
                em.wait("vector", ST, out_busy[par])
                em.op("vector", (lambda i=i, p=par: (lambda e: e.tensor_add(
                    out=out_sb[:, p * DM:(p + 1) * DM], in0=ps_out[:, 0:DM],
                    in1=scrx[:, i * DM:(i + 1) * DM])))(), inc=VP)
                psout_free = em.cnt[VP]
                em.wait("gpsimd", VP, psout_free)
                em.op("gpsimd", (lambda i=i, p=par, t0=t0: (lambda e: e.dma_start(
                    out=out_h[t0 + i * 128: t0 + (i + 1) * 128, :],
                    in_=out_sb[:, p * DM:(p + 1) * DM])))(), inc=ST, k=16)
                out_busy[par] = em.cnt[ST]
            vp_res_prev = em.cnt[VP]
            tp_endI_prev = em.cnt[TP]

        em.wait("gpsimd", ST)

        sems = {name: ctx.enter_context(nc.semaphore(name))
                for name in ("LD", "ST", "TP", "VP", "SP")}
        block = ctx.enter_context(nc.Block())

        def mk(ename):
            def f(eng):
                for item in em.ops[ename]:
                    if item[0] == "wait":
                        _, s, v = item
                        eng.wait_ge(sems[s], v)
                    else:
                        _, fn, inc, k = item
                        inst = fn(eng)
                        if inc is not None:
                            inst.then_inc(sems[inc], k)
            return f

        block.sync(mk("sync"))
        block.tensor(mk("tensor"))
        block.vector(mk("vector"))
        block.scalar(mk("scalar"))
        block.gpsimd(mk("gpsimd"))

    return nc


_CONSTS = None


def _consts():
    global _CONSTS
    if _CONSTS is None:
        selB = np.zeros((2 * NST, NST * 128), dtype=ml_dtypes.bfloat16)
        selC = np.zeros((2 * NST, NST * 128), dtype=ml_dtypes.bfloat16)
        for n in range(NST):
            selB[n, n * 128:(n + 1) * 128] = 1
            selC[NST + n, n * 128:(n + 1) * 128] = 1
        _CONSTS = (selB, selC)
    return _CONSTS


def kernel(L0_attn_x, L0_attn_W_in, L0_attn_conv_w, L0_attn_conv_b, L0_attn_W_x,
           L0_attn_W_dt, L0_attn_b_dt, L0_attn_A_log, L0_attn_D_param,
           L0_attn_W_out, L0_ffn_weight):
    x = np.ascontiguousarray(np.asarray(L0_attn_x, dtype=np.float32))
    selB, selC = _consts()
    shared = {
        "w_in": np.ascontiguousarray(np.asarray(L0_attn_W_in, np.float32)),
        "conv_w": np.ascontiguousarray(np.asarray(L0_attn_conv_w, np.float32).reshape(DI, KC)),
        "conv_b": np.ascontiguousarray(np.asarray(L0_attn_conv_b, np.float32).reshape(DI, 1)),
        "w_x": np.ascontiguousarray(np.asarray(L0_attn_W_x, np.float32)),
        "w_dt": np.ascontiguousarray(np.asarray(L0_attn_W_dt, np.float32)),
        "b_dt": np.ascontiguousarray(np.asarray(L0_attn_b_dt, np.float32).reshape(DI, 1)),
        "a_log": np.ascontiguousarray(np.asarray(L0_attn_A_log, np.float32)),
        "d_par": np.ascontiguousarray(np.asarray(L0_attn_D_param, np.float32).reshape(DI, 1)),
        "w_out": np.ascontiguousarray(np.asarray(L0_attn_W_out, np.float32)),
        "w_ffn": np.ascontiguousarray(np.asarray(L0_ffn_weight, np.float32)),
        "ident": np.eye(128, dtype=ml_dtypes.bfloat16),
        "selB": selB,
        "selC": selC,
        "onesf": np.ones((1, 128), np.float32),
        "onesc": np.ones((128, 1), np.float32),
    }
    in_maps = [dict(shared, x=np.ascontiguousarray(x[b])) for b in range(8)]
    nc = build_program()
    res = run_bass_kernel_spmd(nc, in_maps, core_ids=list(range(8)))
    out = np.stack([np.asarray(r["out"], np.float32) for r in res.results], axis=0)
    return out


if __name__ == "__main__":
    nc = build_program()
    print("built ok")
